# version 03
# baseline (speedup 1.0000x reference)
import contextlib
import math
import os

import numpy as np
import ml_dtypes

import concourse.bass as bass
import concourse.mybir as mybir
from concourse.bass_utils import run_bass_kernel_spmd

F32 = mybir.dt.float32
BF16 = mybir.dt.bfloat16
AF = mybir.ActivationFunctionType
ALU = mybir.AluOpType
AX = mybir.AxisListType

NCORES = 8
D = 2048
H = 1408
DC = D // 128
HC = H // 128
NE = 34
TC = 1024
NG = 2
GR = 512
EPS = 1e-5
DENOM = math.sqrt(2048.0) / 8192.0


class Plan:
    ENGINES = ("tensor", "vector", "scalar", "gpsimd", "sync")

    def __init__(self, nc):
        self.nc = nc
        self.ops = {e: [] for e in self.ENGINES}
        self.cnt = {}

    def sem(self, stack, name):
        s = stack.enter_context(self.nc.semaphore(name))
        self.cnt[name] = 0
        return s

    def emit(self, eng, fn, waits=(), inc=None):
        waits = [(s, v) for (s, v) in waits if v > 0]
        for (s, v) in waits:
            assert v <= self.cnt[s.name], (
                f"wait {s.name}>={v} but only {self.cnt[s.name]} planned")
        after = None
        if inc is not None:
            s, amt = inc
            self.cnt[s.name] += amt
            after = self.cnt[s.name]

        def closure(e, fn=fn, waits=tuple(waits), inc=inc):
            for (s, v) in waits:
                e.wait_ge(s, v)
            ins = fn(e)
            if inc is not None:
                ins.then_inc(inc[0], inc[1])
        self.ops[eng].append(closure)
        return after

    def c(self, sem):
        return self.cnt[sem.name]

    def finalize(self, block):
        for eng in self.ENGINES:
            ops = self.ops[eng]
            getattr(block, eng)(lambda e, ops=ops: [f(e) for f in ops])


def build_program(phase):
    nc = bass.Bass("TRN2", target_bir_lowering=False)
    dr = {}

    def din(name, shape, dt=F32):
        dr[name] = nc.dram_tensor(name, shape, dt, kind="ExternalInput").ap()

    def dout(name, shape, dt=F32):
        dr[name] = nc.dram_tensor(name, shape, dt, kind="ExternalOutput").ap()

    din("xT", [D, TC])
    din("sw1", [2, D, H]); din("sb1", [2, H]); din("sw2", [2, H, D]); din("sb2", [2, D])
    din("rw1", [32, D, H]); din("rb1", [32, H]); din("rw2", [32, H, D]); din("rb2", [32, D])
    din("router_w", [D, 32]); din("router_b", [32, 1])
    din("id_bf", [128, 128], BF16)
    din("id_f32", [128, 128])
    din("sumw_col", [128, 1])
    din("ones_rowf", [1, 128])
    din("ones_sq", [128, 128])
    din("router_b_row", [1, 32])
    din("sel", [NE, NE * 128], BF16)
    DBG = os.environ.get("K_DBG", "0") == "1"
    if DBG:
        dout("xn_out", [D, TC], BF16)
        dout("gates_out", [NE, TC], BF16)
        dout("lg_out", [128, 8 * 32])
        dout("e_out", [128, 8 * 32])
        dout("mask_out", [128, 8 * 32])
        dout("gext_out", [128, 8 * NE], BF16)
        dout("sc_out", [128, 24])
        dout("b1T_out", [128, HC * NE])
        dout("dbg_rows", [1, 4 * GR])
    if phase == "A":
        din("Wk", [D, D]); din("Wv", [D, D])
        dout("x1T_out", [D, TC])
        dout("kv_out", [D, TC], BF16)
        dout("tot_out", [128, DC])
        wk_bfd = nc.dram_tensor("wk_bfd", [D, D], BF16, kind="Internal").ap()
        wv_bfd = nc.dram_tensor("wv_bfd", [D, D], BF16, kind="Internal").ap()
    else:
        din("kv_in", [D, TC], BF16)
        din("tot_all", [128, NCORES * DC])
        din("maskp", [128, NCORES])
        din("Wq", [D, D]); din("Wo", [D, D])
        dout("outT", [D, TC])
        wq_bfd = nc.dram_tensor("wq_bfd", [D, D], BF16, kind="Internal").ap()
        wo_bfd = nc.dram_tensor("wo_bfd", [D, D], BF16, kind="Internal").ap()

    def w1_dram(e):
        return dr["sw1"][e] if e < 2 else dr["rw1"][e - 2]

    def w2_dram(e):
        return dr["sw2"][e] if e < 2 else dr["rw2"][e - 2]

    with contextlib.ExitStack() as st:
        _e = st.enter_context
        pl = Plan(nc)
        sX = pl.sem(st, "sX")
        sW = pl.sem(st, "sW")
        sP = pl.sem(st, "sP")
        sA = pl.sem(st, "sA")
        sV = pl.sem(st, "sV")
        sS = pl.sem(st, "sS")

        xT = _e(nc.sbuf_tensor("xTs", [128, DC, TC], F32))
        xn = _e(nc.sbuf_tensor("xns", [128, DC, TC], BF16))
        id_bf = _e(nc.sbuf_tensor("id_bfs", [128, 128], BF16))
        id_f32 = _e(nc.sbuf_tensor("id_f32s", [128, 128], F32))
        sumw_col = _e(nc.sbuf_tensor("sumw_cols", [128, 1], F32))
        ones_rowf = _e(nc.sbuf_tensor("ones_rowfs", [1, 128], F32))
        ones_sq = _e(nc.sbuf_tensor("ones_sqs", [128, 128], F32))
        rb_row = _e(nc.sbuf_tensor("rb_rows", [1, 32], F32))
        ones_bf = _e(nc.sbuf_tensor("ones_bfs", [128, TC], BF16))
        sel_sb = _e(nc.sbuf_tensor("sel_sbs", [NE, NE * 128], BF16))
        b1T = _e(nc.sbuf_tensor("b1Ts", [128, HC, NE], F32))
        b2_bf = _e(nc.sbuf_tensor("b2_bfs", [NE, D], BF16))
        rw_f = _e(nc.sbuf_tensor("rw_fs", [128, DC, 32], F32))
        cs_bc = _e(nc.sbuf_tensor("cs_bcs", [128, 32], F32))
        b_bc = _e(nc.sbuf_tensor("b_bcs", [128, 32], F32))
        rcol = _e(nc.sbuf_tensor("rcols", [128, 8], F32))
        rmcol = _e(nc.sbuf_tensor("rmcols", [128, 8], F32))
        gatesT = _e(nc.sbuf_tensor("gatesTs", [NE, TC], BF16))
        lgT_sb = _e(nc.sbuf_tensor("lgT_sbs", [32, GR], F32))
        lg_tok = _e(nc.sbuf_tensor("lg_toks", [128, 8, 32], F32))
        lg_work = _e(nc.sbuf_tensor("lg_works", [128, 8, 32], F32))
        lg_mask = _e(nc.sbuf_tensor("lg_masks", [128, 8, 32], F32))
        lg_e = _e(nc.sbuf_tensor("lg_es", [128, 8, 32], F32))
        gates_ext = _e(nc.sbuf_tensor("gates_exts", [128, 8, NE], BF16))
        negmax = _e(nc.sbuf_tensor("negmaxs", [128, 8], F32))
        thr = _e(nc.sbuf_tensor("thrs", [128, 8], F32))
        s6r = _e(nc.sbuf_tensor("s6rs", [128, 8], F32))
        rowbuf = _e(nc.sbuf_tensor("rowbufs", [1, 4 * GR], F32))
        G_sb = _e(nc.sbuf_tensor("G_sbs", [128, 2, GR], BF16))
        ln_f32 = _e(nc.sbuf_tensor("ln_f32s", [128, 2, GR], F32))
        if phase == "B":
            maskp = _e(nc.sbuf_tensor("maskps", [128, NCORES], F32))
            tot_all = _e(nc.sbuf_tensor("tot_alls", [128, NCORES * DC], F32))
            offs = _e(nc.sbuf_tensor("offss", [128, DC], F32))
        else:
            totals = _e(nc.sbuf_tensor("totalss", [128, DC], F32))

        def Lsync(dst, src, waits=()):
            return pl.emit("sync", lambda e, d=dst, s=src: e.dma_start(out=d, in_=s),
                           waits=waits, inc=(sX, 16))

        def Ltrans(dst, src, waits=()):
            return pl.emit("sync",
                           lambda e, d=dst, s=src: e.dma_start_transpose(d, s),
                           waits=waits, inc=(sX, 16))

        def Lgp(dst, src, waits=()):
            return pl.emit("gpsimd", lambda e, d=dst, s=src: e.dma_start(out=d, in_=s),
                           waits=waits, inc=(sW, 16))

        for half in range(2):
            Lsync(xT[:, half * 8:(half + 1) * 8, :],
                  dr["xT"][half * 1024:(half + 1) * 1024, :]
                  .rearrange("(dc p) t -> p dc t", p=128))
        Lsync(id_bf[:], dr["id_bf"][:])
        Lsync(id_f32[:], dr["id_f32"][:])
        Lsync(sumw_col[:], dr["sumw_col"][:])
        Lsync(ones_rowf[:], dr["ones_rowf"][:])
        Lsync(ones_sq[:], dr["ones_sq"][:])
        Lsync(rb_row[:], dr["router_b_row"][:])
        Lsync(sel_sb[:], dr["sel"][:])
        Lsync(rw_f[:], dr["router_w"][:].rearrange("(dc p) r -> p dc r", p=128))
        if phase == "B":
            Lsync(maskp[:], dr["maskp"][:])
            Lsync(tot_all[:], dr["tot_all"][:])
        b1_stack = contextlib.ExitStack()
        b1_all = b1_stack.enter_context(nc.sbuf_tensor("b1_alls", [NE, H], F32))
        Lsync(b1_all[0:2, :], dr["sb1"][:])
        Lsync(b1_all[2:34, :], dr["rb1"][:])
        consts_X = pl.c(sX)
        Lgp(b2_bf[0:2, :], dr["sb2"][:])
        Lgp(b2_bf[2:34, :], dr["rb2"][:])
        consts_W = pl.c(sW)

        v_ones = pl.emit("vector", lambda e: e.memset(ones_bf[:], 1.0), inc=(sV, 1))

        with b1_stack, nc.psum_tensor("ps_b1t", [128, NE], F32) as ps_b1t:
            for hc in range(HC):
                w = [(sX, consts_X)]
                if hc > 0:
                    w.append((sV, pl.c(sV)))
                p_t = pl.emit("tensor",
                              lambda e, hc=hc: nc.tensor.transpose(
                                  ps_b1t[:], b1_all[:, hc * 128:(hc + 1) * 128],
                                  id_f32[0:NE, 0:NE]),
                              waits=w, inc=(sP, 1))
                pl.emit("vector",
                        lambda e, hc=hc: nc.vector.tensor_copy(b1T[:, hc, :], ps_b1t[:]),
                        waits=[(sP, p_t)], inc=(sV, 1))
        b1T_done = pl.c(sV)

        with nc.psum_tensor("ps_cs", [128, 32], F32) as ps_cs:
            for dc in range(DC):
                pl.emit("tensor",
                        lambda e, dc=dc: nc.tensor.matmul(
                            ps_cs[:], ones_sq[:], rw_f[:, dc, :],
                            start=(dc == 0), stop=(dc == DC - 1)),
                        waits=[(sX, consts_X), (sV, b1T_done)] if dc == 0 else [],
                        inc=(sP, 1) if dc == DC - 1 else None)
            p_cs = pl.c(sP)
            pl.emit("vector", lambda e: nc.vector.tensor_copy(cs_bc[:], ps_cs[:]),
                    waits=[(sP, p_cs)], inc=(sV, 1))
            vb = pl.c(sV)
            pb = pl.emit("tensor",
                         lambda e: nc.tensor.matmul(ps_cs[:], ones_rowf[:], rb_row[:],
                                                    start=True, stop=True),
                         waits=[(sV, vb)], inc=(sP, 1))
            pl.emit("vector", lambda e: nc.vector.tensor_copy(b_bc[:], ps_cs[:]),
                    waits=[(sP, pb)], inc=(sV, 1))
        router_done = pl.c(sV)

        def ln(tag):
            after_sp = pl.c(sP)
            after_sv = pl.c(sV)
            with contextlib.ExitStack() as ls:
                en = ls.enter_context
                ps_sum = en(nc.psum_tensor(f"ps_sum_{tag}", [1, GR], F32))
                ps_sq = en(nc.psum_tensor(f"ps_sq_{tag}", [1, GR], F32))
                ps_m = en(nc.psum_tensor(f"ps_m_{tag}", [128, GR], F32))
                ps_r = en(nc.psum_tensor(f"ps_r_{tag}", [128, GR], F32))
                ps_tr = en(nc.psum_tensor(f"ps_tr_{tag}", [128, 1], F32))
                a_rows_prev = 0
                prev_p_sums = 0
                for g in range(NG):
                    t0 = g * GR
                    sq_marks = []
                    for dc in range(DC):
                        sl = dc % 2
                        w = []
                        if dc >= 2:
                            w.append((sP, sq_marks[dc - 2]))
                        elif prev_p_sums:
                            w.append((sP, prev_p_sums))
                        if dc == 0:
                            w.append((sX, consts_X))
                            w.append((sP, after_sp))
                        v2 = pl.emit("vector",
                                     lambda e, dc=dc, sl=sl, t0=t0:
                                     nc.vector.tensor_mul(
                                         ln_f32[:, sl, :], xT[:, dc, t0:t0 + GR],
                                         xT[:, dc, t0:t0 + GR]),
                                     waits=w, inc=(sV, 1))
                        wm = [(sX, consts_X)]
                        if dc == 0 and g == 0:
                            wm.append((sV, after_sv))
                        if dc == 0 and a_rows_prev:
                            wm.append((sA, a_rows_prev))
                        pl.emit("tensor",
                                lambda e, dc=dc, t0=t0: nc.tensor.matmul(
                                    ps_sum[:], sumw_col[:], xT[:, dc, t0:t0 + GR],
                                    start=(dc == 0), stop=(dc == DC - 1)),
                                waits=wm if dc == 0 else [])
                        pl.emit("tensor",
                                lambda e, dc=dc, sl=sl: nc.tensor.matmul(
                                    ps_sq[:], sumw_col[:], ln_f32[:, sl, :],
                                    start=(dc == 0), stop=(dc == DC - 1)),
                                waits=[(sV, v2)], inc=(sP, 1))
                        sq_marks.append(pl.c(sP))
                    p_sums = pl.c(sP)
                    prev_p_sums = p_sums
                    m_ = rowbuf[:, 0:GR]
                    s_ = rowbuf[:, GR:2 * GR]
                    r_ = rowbuf[:, 2 * GR:3 * GR]
                    t_ = rowbuf[:, 3 * GR:4 * GR]
                    a1 = pl.emit("scalar", lambda e: e.copy(m_, ps_sum[:]),
                                 waits=[(sP, p_sums)], inc=(sA, 1))
                    a2 = pl.emit("scalar", lambda e: e.copy(s_, ps_sq[:]),
                                 inc=(sA, 1))
                    a_rows_prev = a2
                    v3 = pl.emit("vector", lambda e: nc.vector.tensor_mul(t_, m_, m_),
                                 waits=[(sA, a2)], inc=(sV, 1))
                    v4 = pl.emit("vector",
                                 lambda e: nc.vector.scalar_tensor_tensor(
                                     s_, s_, EPS, t_, ALU.add, ALU.subtract),
                                 inc=(sV, 1))
                    a3 = pl.emit("scalar", lambda e: e.activation(r_, s_, AF.Ln,
                                                                  bias=0.0, scale=1.0),
                                 waits=[(sV, v4)], inc=(sA, 1))
                    vneg = pl.emit("vector",
                                   lambda e: nc.vector.tensor_scalar_mul(r_, r_, -0.5),
                                   waits=[(sA, a3)], inc=(sV, 1))
                    a4 = pl.emit("scalar", lambda e: e.activation(r_, r_, AF.Exp,
                                                                  bias=0.0, scale=1.0),
                                 waits=[(sV, vneg)], inc=(sA, 1))
                    vrm = pl.emit("vector", lambda e: nc.vector.tensor_mul(t_, m_, r_),
                                  waits=[(sA, a4)], inc=(sV, 1))
                    prev_ev = 0
                    for j in range(8):
                        tt = g * 4 + (j % 4)
                        src_row = r_ if j < 4 else t_
                        dstcol = rcol if j < 4 else rmcol
                        w = [(sA, a4)] if j < 4 else [(sV, vrm)]
                        if prev_ev:
                            w.append((sV, prev_ev))
                        ptr = pl.emit("tensor",
                                      lambda e, j=j, src_row=src_row:
                                      nc.tensor.transpose(
                                          ps_tr[:],
                                          src_row[0:1, (j % 4) * 128:(j % 4 + 1) * 128],
                                          id_f32[0:1, 0:1]),
                                      waits=w, inc=(sP, 1))
                        prev_ev = pl.emit("vector",
                                          lambda e, tt=tt, dstcol=dstcol:
                                          nc.vector.tensor_copy(
                                              dstcol[:, tt:tt + 1], ps_tr[:]),
                                          waits=[(sP, ptr)], inc=(sV, 1))
                    pl.emit("tensor",
                            lambda e: nc.tensor.matmul(ps_m[:], ones_rowf[:], m_,
                                                       start=True, stop=True),
                            waits=[(sA, a4), (sV, pl.c(sV))])
                    pr = pl.emit("tensor",
                                 lambda e: nc.tensor.matmul(ps_r[:], ones_rowf[:], r_,
                                                            start=True, stop=True),
                                 inc=(sP, 1))
                    for dc in range(DC):
                        w = [(sP, pr)]
                        pl.emit("vector",
                                lambda e, dc=dc, t0=t0: nc.vector.tensor_sub(
                                    ln_f32[:, dc % 2, :], xT[:, dc, t0:t0 + GR],
                                    ps_m[:]),
                                waits=w, inc=(sV, 1))
                        pl.emit("vector",
                                lambda e, dc=dc, t0=t0: nc.vector.tensor_mul(
                                    xn[:, dc, t0:t0 + GR], ln_f32[:, dc % 2, :],
                                    ps_r[:]),
                                inc=(sV, 1))
            return pl.c(sV)

        def gating(tag, xn_done):
            with contextlib.ExitStack() as gs:
                en = gs.enter_context
                ps_lg = en(nc.psum_tensor(f"ps_lg_{tag}", [32, GR], F32))
                ps_tp = en(nc.psum_tensor(f"ps_tp_{tag}", [128, 32], F32))
                ps_gt = en(nc.psum_tensor(f"ps_gt_{tag}", [NE, 128], BF16))
                a_lg_prev = 0
                last_tp = 0
                for g in range(NG):
                    t0 = g * GR
                    for dc in range(DC):
                        w = [(sV, xn_done), (sX, consts_X)]
                        if g > 0 and dc == 0:
                            w.append((sA, a_lg_prev))
                        pl.emit("tensor",
                                lambda e, dc=dc, t0=t0: nc.tensor.matmul(
                                    ps_lg[:], rw_f[:, dc, :], xT[:, dc, t0:t0 + GR],
                                    start=(dc == 0), stop=(dc == DC - 1)),
                                waits=w if dc == 0 else [],
                                inc=(sP, 1) if dc == DC - 1 else None)
                    p_lg = pl.c(sP)
                    a_lg = pl.emit("scalar",
                                   lambda e: e.copy(lgT_sb[:], ps_lg[:]),
                                   waits=[(sP, p_lg)] +
                                   ([(sP, last_tp)] if last_tp else []),
                                   inc=(sA, 1))
                    a_lg_prev = a_lg
                    for t in range(4):
                        tt = g * 4 + t
                        w = [(sA, a_lg)]
                        if tt > 0:
                            w.append((sV, pl.c(sV)))
                        pt = pl.emit("tensor",
                                     lambda e, t=t: nc.tensor.transpose(
                                         ps_tp[:], lgT_sb[:, t * 128:(t + 1) * 128],
                                         id_f32[0:32, 0:32]),
                                     waits=w, inc=(sP, 1))
                        pl.emit("vector",
                                lambda e, tt=tt: nc.vector.tensor_copy(
                                    lg_tok[:, tt, :], ps_tp[:]),
                                waits=[(sP, pt)], inc=(sV, 1))
                    last_tp = pl.c(sP)
                lg_ready = pl.c(sV)
                for tt in range(8):
                    w = [(sV, lg_ready)] if tt == 0 else []
                    pl.emit("vector",
                            lambda e, tt=tt: nc.vector.tensor_scalar(
                                lg_tok[:, tt, :], lg_tok[:, tt, :],
                                rcol[:, tt:tt + 1], None, ALU.mult),
                            waits=w)
                    pl.emit("vector",
                            lambda e, tt=tt: nc.vector.scalar_tensor_tensor(
                                lg_work[:, tt, :], cs_bc[:],
                                rmcol[:, tt:tt + 1], b_bc[:],
                                ALU.mult, ALU.subtract),
                            waits=[])
                    pl.emit("vector",
                            lambda e, tt=tt: nc.vector.tensor_sub(
                                lg_tok[:, tt, :], lg_tok[:, tt, :],
                                lg_work[:, tt, :]),
                            inc=(sV, 1) if tt == 7 else None)
                lg_ready = pl.c(sV)
                pl.emit("vector",
                        lambda e: nc.vector.reduce_max(negmax[:], lg_tok[:], AX.X,
                                                       negate=True),
                        waits=[(sV, lg_ready)], inc=(sV, 1))
                pl.emit("vector", lambda e: nc.vector.tensor_copy(lg_work[:], lg_tok[:]),
                        inc=(sV, 1))
                prev_stt = 0
                for it in range(6):
                    vr = pl.emit("vector",
                                 lambda e: nc.vector.reduce_max(thr[:], lg_work[:],
                                                                AX.X),
                                 waits=[(sV, prev_stt)] if prev_stt else [],
                                 inc=(sV, 1))
                    if it == 5:
                        break
                    for t in range(8):
                        pl.emit("vector",
                                lambda e, t=t: nc.vector.tensor_scalar(
                                    lg_mask[:, t, :], lg_work[:, t, :],
                                    thr[:, t:t + 1], None, ALU.is_equal),
                                waits=[(sV, vr)] if t == 0 else [])
                    prev_stt = pl.emit("vector",
                                       lambda e: nc.vector.scalar_tensor_tensor(
                                           lg_work[:], lg_mask[:], -1e30, lg_work[:],
                                           ALU.mult, ALU.add),
                                       inc=(sV, 1))
                thr_done = pl.c(sV)
                for t in range(8):
                    pl.emit("scalar",
                            lambda e, t=t: e.activation(
                                lg_e[:, t, :], lg_tok[:, t, :], AF.Exp,
                                bias=negmax[:, t:t + 1], scale=1.0),
                            waits=[(sV, thr_done)] if t == 0 else [],
                            inc=(sA, 1) if t == 7 else None)
                a_exp = pl.c(sA)
                for t in range(8):
                    pl.emit("vector",
                            lambda e, t=t: nc.vector.tensor_scalar(
                                lg_mask[:, t, :], lg_tok[:, t, :],
                                thr[:, t:t + 1], None, ALU.is_ge),
                            waits=[(sV, thr_done)] if t == 0 else [])
                vgm = pl.emit("vector",
                              lambda e: nc.vector.tensor_mul(lg_e[:], lg_e[:],
                                                             lg_mask[:]),
                              waits=[(sA, a_exp)], inc=(sV, 1))
                vred = pl.emit("vector",
                               lambda e: nc.vector.tensor_reduce(s6r[:], lg_e[:], AX.X,
                                                                 ALU.add),
                               waits=[(sV, vgm)], inc=(sV, 1))
                vrc = pl.emit("vector", lambda e: nc.vector.reciprocal(s6r[:], s6r[:]),
                              waits=[(sV, vred)], inc=(sV, 1))
                pl.emit("vector", lambda e: e.memset(gates_ext[:], 0.25), inc=(sV, 1))
                for t in range(8):
                    pl.emit("vector",
                            lambda e, t=t: nc.vector.tensor_scalar(
                                gates_ext[:, t, 2:NE], lg_e[:, t, :],
                                s6r[:, t:t + 1], 0.5, ALU.mult, ALU.mult),
                            waits=[(sV, vrc)] if t == 0 else [],
                            inc=(sV, 1) if t == 7 else None)
                g_ext = pl.c(sV)
                for t in range(8):
                    w = [(sV, g_ext)]
                    if t > 0:
                        w.append((sV, pl.c(sV)))
                    pt = pl.emit("tensor",
                                 lambda e, t=t: nc.tensor.transpose(
                                     ps_gt[:], gates_ext[:, t, :], id_bf[:]),
                                 waits=w, inc=(sP, 1))
                    pl.emit("vector",
                            lambda e, t=t, pt=pt: nc.vector.tensor_copy(
                                gatesT[:, t * 128:(t + 1) * 128], ps_gt[:]),
                            waits=[(sP, pt)], inc=(sV, 1))
                return pl.c(sV)

        def ffn_call(tag, xn_done, gates_done):
            with contextlib.ExitStack() as fs:
                en = fs.enter_context
                w1r = en(nc.sbuf_tensor(f"w1r_{tag}", [128, 3, DC, 128], BF16))
                w2r = en(nc.sbuf_tensor(f"w2r_{tag}", [128, 2, HC, 256], BF16))
                hT = en(nc.sbuf_tensor(f"hT_{tag}", [128, HC, TC], BF16))
                ps_h = en(nc.psum_tensor(f"ps_h_{tag}", [128, 2, GR], F32))
                ps_G = en(nc.psum_tensor(f"ps_G_{tag}", [128, GR], F32))
                ps_acc = en(nc.psum_tensor(f"ps_acc_{tag}", [128, 4, GR], F32))

                guard = pl.c(sP)
                w1_ready = {}
                w2_ready = {}
                w1_lastuse = [guard] * 3
                w2_lastuse = [guard] * 2
                h_evac = []
                g_free = [0, 0]
                acc_free = [0] * 4
                last_G_evac = 0
                hT_free = 0

                def load_w1(e_, hc):
                    slot = (e_ * HC + hc) % 3
                    w = [(sP, w1_lastuse[slot])]
                    src = w1_dram(e_)
                    for half in range(2):
                        pl.emit("gpsimd",
                                lambda en_, src=src, hc=hc, slot=slot, half=half:
                                en_.dma_start(
                                    out=w1r[:, slot, half * 8:(half + 1) * 8, :],
                                    in_=src[half * 1024:(half + 1) * 1024,
                                            hc * 128:(hc + 1) * 128]
                                    .rearrange("(dc p) h -> p dc h", p=128)),
                                waits=w if half == 0 else [],
                                inc=(sW, 16))
                    w1_ready[(e_, hc)] = pl.c(sW)
                    return slot

                def load_w2(e_, q):
                    slot = q % 2
                    w = [(sP, w2_lastuse[slot])]
                    src = w2_dram(e_)
                    for half in range(2):
                        r0 = half * 768
                        nhc = 6 if half == 0 else 5
                        pl.emit("gpsimd",
                                lambda en_, src=src, q=q, slot=slot, r0=r0, nhc=nhc,
                                half=half:
                                en_.dma_start(
                                    out=w2r[:, slot, half * 6:half * 6 + nhc, :],
                                    in_=src[r0:r0 + nhc * 128, q * 256:(q + 1) * 256]
                                    .rearrange("(hc p) d -> p hc d", p=128)),
                                waits=w if half == 0 else [],
                                inc=(sW, 16))
                    w2_ready[(e_, q)] = pl.c(sW)
                    return slot

                for e_ in range(NE):
                    G_ready = {}
                    for g in range(NG):
                        w = [(sV, gates_done), (sX, consts_X),
                             (sA, last_G_evac), (sV, g_free[g])]
                        pg = pl.emit(
                            "tensor",
                            lambda e, e_=e_, g=g: nc.tensor.matmul(
                                ps_G[:], sel_sb[:, e_ * 128:(e_ + 1) * 128],
                                gatesT[:, g * GR:(g + 1) * GR],
                                start=True, stop=True),
                            waits=w, inc=(sP, 1))
                        ag = pl.emit(
                            "scalar",
                            lambda e, g=g: e.copy(G_sb[:, g, :], ps_G[:]),
                            waits=[(sP, pg)], inc=(sA, 1))
                        last_G_evac = ag
                        G_ready[g] = ag
                    hG = {}
                    for hc in range(HC):
                        w1slot = load_w1(e_, hc)
                        for g in range(NG):
                            par = len(h_evac) % 2
                            w = [(sW, w1_ready[(e_, hc)]), (sV, xn_done),
                                 (sV, b1T_done), (sV, gates_done)]
                            if len(h_evac) >= 2:
                                w.append((sA, h_evac[-2]))
                            if hT_free:
                                w.append((sP, hT_free))
                            for dc in range(DC):
                                pl.emit(
                                    "tensor",
                                    lambda e, dc=dc, g=g, par=par, w1slot=w1slot:
                                    nc.tensor.matmul(
                                        ps_h[:, par, :],
                                        w1r[:, w1slot, dc, :],
                                        xn[:, dc, g * GR:(g + 1) * GR],
                                        start=(dc == 0), stop=(dc == DC - 1)),
                                    waits=w if dc == 0 else [],
                                    inc=(sP, 1) if dc == DC - 1 else None)
                            p_h = pl.c(sP)
                            w1_lastuse[w1slot] = p_h
                            ah = pl.emit(
                                "scalar",
                                lambda e, hc=hc, g=g, par=par, e_=e_:
                                e.activation(
                                    hT[:, hc, g * GR:(g + 1) * GR],
                                    ps_h[:, par, :], AF.Relu,
                                    bias=b1T[:, hc, e_:e_ + 1], scale=1.0),
                                waits=[(sP, p_h), (sV, b1T_done)],
                                inc=(sA, 1))
                            h_evac.append(ah)
                            vh = pl.emit(
                                "vector",
                                lambda e, hc=hc, g=g:
                                nc.vector.tensor_mul(
                                    hT[:, hc, g * GR:(g + 1) * GR],
                                    hT[:, hc, g * GR:(g + 1) * GR],
                                    G_sb[:, g, :]),
                                waits=[(sA, max(ah, G_ready[g]))],
                                inc=(sV, 1))
                            hG[(hc, g)] = vh
                    for g in range(NG):
                        g_free[g] = max(hG[(hc, g)] for hc in range(HC))
                    for dc in range(DC):
                        q, dcq = divmod(dc, 2)
                        if dcq == 0:
                            w2slot = load_w2(e_, q)
                        for g in range(NG):
                            aslot = (dc * NG + g) % 4
                            wf = [(sV, max(hG[(hc, g)] for hc in range(HC))),
                                  (sW, w2_ready[(e_, q)]),
                                  (sV, acc_free[aslot]),
                                  (sV, gates_done), (sW, consts_W)]
                            if e_ == 0:
                                pl.emit(
                                    "tensor",
                                    lambda e, dc=dc, g=g, aslot=aslot:
                                    nc.tensor.matmul(
                                        ps_acc[:, aslot, :],
                                        b2_bf[:, dc * 128:(dc + 1) * 128],
                                        gatesT[:, g * GR:(g + 1) * GR],
                                        start=True, stop=False),
                                    waits=wf)
                                wf = None
                            for hc in range(HC):
                                pl.emit(
                                    "tensor",
                                    lambda e, hc=hc, dcq=dcq, g=g, aslot=aslot,
                                    w2slot=w2slot,
                                    stt=(hc == 0 and e_ > 0),
                                    stp=(hc == HC - 1):
                                    nc.tensor.matmul(
                                        ps_acc[:, aslot, :],
                                        w2r[:, w2slot, hc, dcq * 128:(dcq + 1) * 128],
                                        hT[:, hc, g * GR:(g + 1) * GR],
                                        start=stt, stop=stp),
                                    waits=(wf if (hc == 0 and wf is not None) else []),
                                    inc=(sP, 1) if hc == HC - 1 else None)
                            p2 = pl.c(sP)
                            w2_lastuse[w2slot] = p2
                            va = pl.emit(
                                "vector",
                                lambda e, dc=dc, g=g, aslot=aslot:
                                nc.vector.tensor_add(
                                    xT[:, dc, g * GR:(g + 1) * GR],
                                    xT[:, dc, g * GR:(g + 1) * GR],
                                    ps_acc[:, aslot, :]),
                                waits=[(sP, p2)], inc=(sV, 1))
                            acc_free[aslot] = va
                    hT_free = pl.c(sP)
                return pl.c(sV)

        def cast_weight(w_src, w_dst):
            for part in range(4):
                Lgp(w_dst[part * 512:(part + 1) * 512, :],
                    w_src[part * 512:(part + 1) * 512, :])
            return pl.c(sW)

        def store(dst, src, waits):
            return pl.emit("gpsimd",
                           lambda e, d=dst, s=src: e.dma_start(out=d, in_=s),
                           waits=waits, inc=(sS, 16))

        if phase == "A":
            xn1 = ln("ln1")
            if DBG:
                for half in range(2):
                    store(dr["xn_out"][half * 1024:(half + 1) * 1024, :]
                          .rearrange("(dc p) t -> p dc t", p=128),
                          xn[:, half * 8:(half + 1) * 8, :], [(sV, xn1)])
                store(dr["dbg_rows"][:], rowbuf[:], [(sV, xn1)])
                store(dr["b1T_out"][:].rearrange("p (hc n) -> p hc n", n=NE),
                      b1T[:], [(sV, b1T_done)])
            g1 = gating("g1", xn1)
            if DBG:
                store(dr["gates_out"][:], gatesT[:], [(sV, g1)])
                store(dr["lg_out"][:].rearrange("p (t c) -> p t c", c=32),
                      lg_tok[:], [(sV, g1)])
                store(dr["e_out"][:].rearrange("p (t c) -> p t c", c=32),
                      lg_e[:], [(sV, g1)])
                store(dr["mask_out"][:].rearrange("p (t c) -> p t c", c=32),
                      lg_mask[:], [(sV, g1)])
                store(dr["gext_out"][:].rearrange("p (t c) -> p t c", c=NE),
                      gates_ext[:], [(sV, g1)])
                store(dr["sc_out"][:, 0:8], negmax[:], [(sV, g1)])
                store(dr["sc_out"][:, 8:16], thr[:], [(sV, g1)])
                store(dr["sc_out"][:, 16:24], s6r[:], [(sV, g1)])
            f1 = ffn_call("f1", xn1, g1)
            xn2 = ln("ln2")
            ck = cast_weight(dr["Wk"], wk_bfd)
            cv = cast_weight(dr["Wv"], wv_bfd)
            with contextlib.ExitStack() as ats:
                en = ats.enter_context
                kvv = en(nc.sbuf_tensor("kvvA", [128, DC, TC], BF16))
                wtile = en(nc.sbuf_tensor("wtileA", [128, 3, D], BF16,
                                          is_transpose_dest=True))
                k_st = en(nc.sbuf_tensor("k_stA", [128, 2, TC], F32))
                v_st = en(nc.sbuf_tensor("v_stA", [128, 2, TC], F32))
                ps_at = en(nc.psum_tensor("ps_atA", [128, 8, GR], F32))
                guard_at = pl.c(sP)
                wt_next = [0]
                wt_lastuse = [guard_at] * 3
                ev = 0
                kv_marks = []
                NE8 = int(os.environ.get("K_E8", "8"))
                if NE8 < 8:
                    pl.emit("vector", lambda e: e.memset(
                        kvv[:].rearrange("p a b -> p (a b)"), 0.0), inc=(sV, 1))
                    pl.emit("vector", lambda e: e.memset(totals[:], 0.0), inc=(sV, 1))
                for e8 in range(NE8):
                    for dc in range(DC):
                        tls = []
                        for wsrc in (wk_bfd, wv_bfd):
                            tslot = wt_next[0] % 3
                            wt_next[0] += 1
                            tl = Ltrans(wtile[:, tslot, :],
                                        wsrc[:, dc * 128:(dc + 1) * 128],
                                        waits=[(sW, cv), (sP, wt_lastuse[tslot])])
                            tls.append((tslot, tl))
                        for iw, (tslot, tl) in enumerate(tls):
                            for i_ec in range(2):
                                ec = e8 * 2 + i_ec
                                for g in range(NG):
                                    bank = iw * 4 + i_ec * 2 + g
                                    w = [(sX, tl), (sV, xn2)]
                                    if dc == 0:
                                        w.append((sA, ev))
                                    pl.emit(
                                        "tensor",
                                        lambda e, dc=dc, ec=ec, g=g, bank=bank,
                                        tslot=tslot: nc.tensor.matmul(
                                            ps_at[:, bank, :],
                                            wtile[:, tslot, ec * 128:(ec + 1) * 128],
                                            xn[:, dc, g * GR:(g + 1) * GR],
                                            start=(dc == 0), stop=(dc == DC - 1)),
                                        waits=w,
                                        inc=(sP, 1) if bank == 7 else None)
                        for (ts_, _tl) in tls:
                            wt_lastuse[ts_] = pl.c(sP)
                    p_q = pl.c(sP)
                    for iw in range(2):
                        dst = k_st if iw == 0 else v_st
                        for i_ec in range(2):
                            for g in range(NG):
                                bank = iw * 4 + i_ec * 2 + g
                                pl.emit("scalar",
                                        lambda e, i_ec=i_ec, g=g, bank=bank, dst=dst:
                                        e.copy(
                                            dst[:, i_ec, g * GR:(g + 1) * GR],
                                            ps_at[:, bank, :]),
                                        waits=[(sP, p_q)], inc=(sA, 1))
                    ev = pl.c(sA)
                    for i_ec in range(2):
                        ec = e8 * 2 + i_ec
                        pl.emit("vector",
                                lambda e, ec=ec, i_ec=i_ec:
                                nc.vector.scalar_tensor_tensor(
                                    kvv[:, ec, :], k_st[:, i_ec, :], DENOM,
                                    v_st[:, i_ec, :], ALU.mult, ALU.mult),
                                waits=[(sA, ev)], inc=(sV, 1))
                        pl.emit("vector",
                                lambda e, ec=ec: nc.vector.tensor_reduce(
                                    totals[:, ec:ec + 1], kvv[:, ec, :], AX.X,
                                    ALU.add),
                                waits=[(sV, pl.c(sV))], inc=(sV, 1))
                kv_done = pl.c(sV)
                for half in range(2):
                    store(dr["x1T_out"][half * 1024:(half + 1) * 1024, :]
                          .rearrange("(dc p) t -> p dc t", p=128),
                          xT[:, half * 8:(half + 1) * 8, :], [(sV, f1)])
                    store(dr["kv_out"][half * 1024:(half + 1) * 1024, :]
                          .rearrange("(dc p) t -> p dc t", p=128),
                          kvv[:, half * 8:(half + 1) * 8, :], [(sV, kv_done)])
                store(dr["tot_out"][:], totals[:], [(sV, kv_done)])
                pl.emit("gpsimd", lambda e: e.engine_nop(), waits=[(sS, pl.c(sS))])
        else:
            pl.emit("vector", lambda e: e.memset(offs[:], 0.0),
                    waits=[(sX, consts_X)], inc=(sV, 1))
            for r in range(NCORES):
                pl.emit("vector",
                        lambda e, r=r: nc.vector.scalar_tensor_tensor(
                            offs[:], tot_all[:, r * DC:(r + 1) * DC],
                            maskp[:, r:r + 1], offs[:], ALU.mult, ALU.add),
                        waits=[(sV, pl.c(sV))], inc=(sV, 1))
            offs_done = pl.c(sV)
            xn2 = ln("ln2")
            cq = cast_weight(dr["Wq"], wq_bfd)
            co = cast_weight(dr["Wo"], wo_bfd)
            with contextlib.ExitStack() as ats:
                en = ats.enter_context
                kvv = en(nc.sbuf_tensor("kvvB", [128, DC, TC], BF16))
                wtile = en(nc.sbuf_tensor("wtileB", [128, 3, D], BF16,
                                          is_transpose_dest=True))
                q_st = en(nc.sbuf_tensor("q_stB", [128, 2, TC], F32))
                kvt_st = en(nc.sbuf_tensor("kvt_stB", [128, TC], F32))
                kv_stage = en(nc.sbuf_tensor("kv_stageB", [128, 2, TC], BF16))
                ps_at = en(nc.psum_tensor("ps_atB", [128, 8, GR], F32))
                guard_at = pl.c(sP)
                wt_next = [0]
                wt_lastuse = [guard_at] * 3
                ev = 0
                vhat_done = {}
                kvld_prev = 0
                for e8 in range(8):
                    kvld = Lsync(kv_stage[:],
                                 dr["kv_in"][e8 * 256:(e8 + 1) * 256, :]
                                 .rearrange("(i p) t -> p i t", p=128),
                                 waits=[(sV, kvld_prev), (sP, guard_at)])
                    for dc in range(DC):
                        tslot = wt_next[0] % 3
                        wt_next[0] += 1
                        tl = Ltrans(wtile[:, tslot, :],
                                    wq_bfd[:, dc * 128:(dc + 1) * 128],
                                    waits=[(sW, co), (sP, wt_lastuse[tslot])])
                        for i_ec in range(2):
                            ec = e8 * 2 + i_ec
                            for g in range(NG):
                                bank = i_ec * 2 + g
                                w = [(sX, tl), (sV, xn2)]
                                if dc == 0:
                                    w.append((sA, ev))
                                pl.emit(
                                    "tensor",
                                    lambda e, dc=dc, ec=ec, g=g, bank=bank,
                                    tslot=tslot: nc.tensor.matmul(
                                        ps_at[:, bank, :],
                                        wtile[:, tslot, ec * 128:(ec + 1) * 128],
                                        xn[:, dc, g * GR:(g + 1) * GR],
                                        start=(dc == 0), stop=(dc == DC - 1)),
                                    waits=w,
                                    inc=(sP, 1) if bank == 3 else None)
                        wt_lastuse[tslot] = pl.c(sP)
                    p_q = pl.c(sP)
                    for i_ec in range(2):
                        for g in range(NG):
                            bank = i_ec * 2 + g
                            pl.emit("scalar",
                                    lambda e, i_ec=i_ec, g=g, bank=bank: e.copy(
                                        q_st[:, i_ec, g * GR:(g + 1) * GR],
                                        ps_at[:, bank, :]),
                                    waits=[(sP, p_q)], inc=(sA, 1))
                    ev = pl.c(sA)
                    for i_ec in range(2):
                        ec = e8 * 2 + i_ec
                        pl.emit("vector",
                                lambda e, i_ec=i_ec, ec=ec:
                                nc.vector.tensor_tensor_scan(
                                    kvt_st[:], ones_bf[:], kv_stage[:, i_ec, :],
                                    offs[:, ec:ec + 1], ALU.mult, ALU.add),
                                waits=[(sV, offs_done), (sX, kvld), (sV, v_ones)],
                                inc=(sV, 1))
                        vd = pl.emit("vector",
                                     lambda e, i_ec=i_ec, ec=ec:
                                     nc.vector.tensor_mul(
                                         kvv[:, ec, :], q_st[:, i_ec, :], kvt_st[:]),
                                     waits=[(sA, ev)], inc=(sV, 1))
                        vhat_done[ec] = vd
                    kvld_prev = pl.c(sV)
                ev_final = ev
                attn_add = 0
                for halff in range(2):
                    for g in range(NG):
                        for ec in range(DC):
                            tslot = wt_next[0] % 3
                            wt_next[0] += 1
                            tl = Ltrans(wtile[:, tslot, :],
                                        wo_bfd[:, ec * 128:(ec + 1) * 128],
                                        waits=[(sW, co), (sP, wt_lastuse[tslot])])
                            for f8 in range(8):
                                fc = halff * 8 + f8
                                w = [(sX, tl), (sV, vhat_done[ec])]
                                if ec == 0:
                                    w += [(sV, attn_add), (sA, ev_final)]
                                pl.emit(
                                    "tensor",
                                    lambda e, ec=ec, fc=fc, f8=f8, g=g, tslot=tslot:
                                    nc.tensor.matmul(
                                        ps_at[:, f8, :],
                                        wtile[:, tslot, fc * 128:(fc + 1) * 128],
                                        kvv[:, ec, g * GR:(g + 1) * GR],
                                        start=(ec == 0), stop=(ec == DC - 1)),
                                    waits=w,
                                    inc=(sP, 1) if f8 == 7 else None)
                            wt_lastuse[tslot] = pl.c(sP)
                        p_o = pl.c(sP)
                        for f8 in range(8):
                            fc = halff * 8 + f8
                            pl.emit("vector",
                                    lambda e, fc=fc, f8=f8, g=g:
                                    nc.vector.tensor_add(
                                        xT[:, fc, g * GR:(g + 1) * GR],
                                        xT[:, fc, g * GR:(g + 1) * GR],
                                        ps_at[:, f8, :]),
                                    waits=[(sP, p_o)], inc=(sV, 1))
                        attn_add = pl.c(sV)
            xn3 = ln("ln3")
            g2 = gating("g2", xn3)
            f2 = ffn_call("f2", xn3, g2)
            for half in range(2):
                store(dr["outT"][half * 1024:(half + 1) * 1024, :]
                      .rearrange("(dc p) t -> p dc t", p=128),
                      xT[:, half * 8:(half + 1) * 8, :], [(sV, f2)])
            pl.emit("gpsimd", lambda e: e.engine_nop(), waits=[(sS, pl.c(sS))])

        with nc.Block() as block:
            pl.finalize(block)
    return nc


_CACHE = {}


def _consts():
    if "c" in _CACHE:
        return _CACHE["c"]
    sel = np.zeros((NE, NE * 128), dtype=ml_dtypes.bfloat16)
    for e in range(NE):
        sel[e, e * 128:(e + 1) * 128] = 1
    c = dict(
        id_bf=np.eye(128, dtype=ml_dtypes.bfloat16),
        id_f32=np.eye(128, dtype=np.float32),
        sumw_col=np.full((128, 1), 1.0 / 2048.0, dtype=np.float32),
        ones_rowf=np.ones((1, 128), dtype=np.float32),
        ones_sq=np.ones((128, 128), dtype=np.float32),
        sel=sel,
    )
    _CACHE["c"] = c
    return c


def kernel(L0_attn_x, L0_attn_Wq, L0_attn_Wk, L0_attn_Wv, L0_attn_Wo,
           L0_ffn_sw1, L0_ffn_sb1, L0_ffn_sw2, L0_ffn_sb2,
           L0_ffn_rw1, L0_ffn_rb1, L0_ffn_rw2, L0_ffn_rb2,
           L0_ffn_router_w, L0_ffn_router_b):
    x = np.asarray(L0_attn_x)
    consts = _consts()
    wshared = dict(
        sw1=np.ascontiguousarray(np.asarray(L0_ffn_sw1, dtype=np.float32)),
        sb1=np.ascontiguousarray(np.asarray(L0_ffn_sb1, dtype=np.float32)),
        sw2=np.ascontiguousarray(np.asarray(L0_ffn_sw2, dtype=np.float32)),
        sb2=np.ascontiguousarray(np.asarray(L0_ffn_sb2, dtype=np.float32)),
        rw1=np.ascontiguousarray(np.asarray(L0_ffn_rw1, dtype=np.float32)),
        rb1=np.ascontiguousarray(np.asarray(L0_ffn_rb1, dtype=np.float32)),
        rw2=np.ascontiguousarray(np.asarray(L0_ffn_rw2, dtype=np.float32)),
        rb2=np.ascontiguousarray(np.asarray(L0_ffn_rb2, dtype=np.float32)),
        router_w=np.ascontiguousarray(np.asarray(L0_ffn_router_w, dtype=np.float32)),
        router_b=np.ascontiguousarray(
            np.asarray(L0_ffn_router_b, dtype=np.float32).reshape(32, 1)),
        router_b_row=np.ascontiguousarray(
            np.asarray(L0_ffn_router_b, dtype=np.float32).reshape(1, 32)),
        **consts,
    )

    if "ncA" not in _CACHE:
        _CACHE["ncA"] = build_program("A")
        _CACHE["ncB"] = build_program("B")

    in_A = []
    for c in range(NCORES):
        xT = np.ascontiguousarray(
            np.asarray(x[0, c * TC:(c + 1) * TC, :], dtype=np.float32).T)
        in_A.append(dict(
            xT=xT,
            Wk=np.ascontiguousarray(np.asarray(L0_attn_Wk, dtype=np.float32)),
            Wv=np.ascontiguousarray(np.asarray(L0_attn_Wv, dtype=np.float32)),
            **wshared))
    resA = run_bass_kernel_spmd(_CACHE["ncA"], in_A, core_ids=list(range(NCORES)))

    tot_all = np.ascontiguousarray(np.concatenate(
        [np.asarray(resA.results[r]["tot_out"]) for r in range(NCORES)], axis=1))
    in_B = []
    for c in range(NCORES):
        maskp = np.ascontiguousarray(
            np.tile((np.arange(NCORES) < c).astype(np.float32), (128, 1)))
        in_B.append(dict(
            xT=np.ascontiguousarray(np.asarray(resA.results[c]["x1T_out"])),
            kv_in=np.ascontiguousarray(np.asarray(resA.results[c]["kv_out"])),
            tot_all=tot_all, maskp=maskp,
            Wq=np.ascontiguousarray(np.asarray(L0_attn_Wq, dtype=np.float32)),
            Wo=np.ascontiguousarray(np.asarray(L0_attn_Wo, dtype=np.float32)),
            **wshared))
    resB = run_bass_kernel_spmd(_CACHE["ncB"], in_B, core_ids=list(range(NCORES)))

    out = np.concatenate(
        [np.asarray(resB.results[c]["outT"], dtype=np.float32).T
         for c in range(NCORES)], axis=0)
    return np.ascontiguousarray(out.reshape(1, 8192, D))
